# version 01
# baseline (speedup 1.0000x reference)
import numpy as np
import ml_dtypes
import concourse.bass as bass
import concourse.mybir as mybir
from concourse.bass_utils import run_bass_kernel_spmd
from contextlib import ExitStack
import types

F32 = mybir.dt.float32
BF16 = mybir.dt.bfloat16
AF = mybir.ActivationFunctionType
ALU = mybir.AluOpType
AX = mybir.AxisListType

NCORES = 8
D = 1024
DC = 8
SEQ = 8192
OWN = 4096
HALO = 128
NT = OWN + HALO
EPS = 1e-5

ENGINES = ("gpsimd", "sync", "tensor", "scalar", "vector")


class Sem:
    __slots__ = ("h", "n")

    def __init__(self, h):
        self.h = h
        self.n = 0

    def inc(self, amt):
        self.n += amt
        return self.n


class B:

    def __init__(self):
        self.nc = bass.Bass("TRN2", target_bir_lowering=False)
        self.ctx = ExitStack()
        self.steps = {e: [] for e in ENGINES}
        self.last_wait = {e: {} for e in ENGINES}
        self.arena = self.nc.auto_arena()
        self.qgp = self._sem("qgp")
        self.qsy = self._sem("qsy")
        self.spe = self._sem("spe")
        self.sact = self._sem("sact")
        self.sdve = self._sem("sdve")
        self.banks = [self.ctx.enter_context(
            self.nc.psum_tensor(f"bank{i}", [128, 512], F32)) for i in range(8)]
        self.bank_dep = [[] for _ in range(8)]

    def _sem(self, name):
        return Sem(self.ctx.enter_context(self.nc.semaphore(name)))

    def S(self, name):
        if not hasattr(self, "sem_cache"):
            self.sem_cache = {}
        if name not in self.sem_cache:
            self.sem_cache[name] = self._sem(name)
        return self.sem_cache[name]

    def final_dma_waits(self, eng="sync"):
        for sem in getattr(self, "sem_cache", {}).values():
            self.wait(eng, sem, sem.n)
        self.wait(eng, self.qsy, self.qsy.n)
        self.wait(eng, self.qgp, self.qgp.n)

    def sb(self, name, shape, dtype=F32):
        return self.arena.sbuf_tensor(name, list(shape), dtype)

    def wait(self, eng, sem, val):
        if val is None or val <= 0:
            return
        lw = self.last_wait[eng].get(id(sem), 0)
        if val <= lw:
            return
        self.last_wait[eng][id(sem)] = val
        self.steps[eng].append(lambda e, s=sem.h, v=val: e.wait_ge(s, v))

    def op(self, eng, fn, inc=None, amt=1):
        if inc is None:
            self.steps[eng].append(fn)
            return None
        v = inc.inc(amt)
        self.steps[eng].append(lambda e, f=fn, s=inc.h, a=amt: f(e).then_inc(s, a))
        return v

    def dma(self, eng, out, in_, sem=None, **kw):
        s = sem if sem is not None else (self.qgp if eng == "gpsimd" else self.qsy)
        return self.op(eng, lambda e, o=out, i=in_, kw=kw: e.dma_start(out=o, in_=i, **kw),
                       inc=s, amt=16)

    def use_bank(self, i, eng="tensor"):
        for (s, v) in self.bank_dep[i]:
            self.wait(eng, s, v)
        self.bank_dep[i] = []
        return self.banks[i]

    def release_bank(self, i, sem, val):
        self.bank_dep[i].append((sem, val))

    def overlay(self, *fns):
        base = self.arena.addr
        end = base
        outs = []
        for fn in fns:
            self.arena.addr = base
            outs.append(fn())
            end = max(end, self.arena.addr)
        self.arena.addr = end
        return outs

    def phase_gate(self, engines=("tensor", "scalar", "vector", "sync", "gpsimd")):
        sems = [self.spe, self.sact, self.sdve, self.qsy, self.qgp]
        sems += list(getattr(self, "sem_cache", {}).values())
        for eng in engines:
            for sem in sems:
                self.wait(eng, sem, sem.n)

    def finalize(self):
        self.arena.allocate()
        assert self.arena.size <= 212000, f"SBUF over budget: {self.arena.size}"
        with self.nc.Block() as block:
            for eng in ENGINES:
                def run(e, eng=eng):
                    for f in self.steps[eng]:
                        f(e)
                getattr(block, eng)(run)
        return self.nc


def groups_of(total, T, first=None):
    out = []
    off = 0
    if first:
        out.append((0, first))
        off = first
    while off < total:
        w = min(T, total - off)
        out.append((off, w))
        off += w
    return out


def nslices(T):
    out = []
    off = 0
    while off < T:
        w = min(512, T - off)
        out.append((off, w))
        off += w
    return out


def fm(ap):
    return ap.rearrange("(c p) t -> p c t", p=128)



def emit_globals(b):
    nc = b.nc
    g = {}
    g["ones_col"] = b.sb("g_ones_col", [128, 1], F32)
    g["ones_row_f"] = b.sb("g_ones_row_f", [1, 512], F32)
    g["ones_row_b"] = b.sb("g_ones_row_b", [1, 512], BF16)
    g["ident"] = b.sb("g_ident", [128, 128], F32)
    g["onehot"] = b.sb("g_onehot", [8, 8 * 128], BF16)
    ident_d = nc.inline_tensor(np.eye(128, dtype=np.float32), "identc").ap()
    oh = np.zeros((8, 8 * 128), dtype=ml_dtypes.bfloat16)
    for e in range(8):
        oh[e, e * 128:(e + 1) * 128] = 1.0
    oh_d = nc.inline_tensor(oh, "onehotc").ap()
    b.op("gpsimd", lambda e: e.memset(g["ones_col"][:], 1.0))
    b.op("gpsimd", lambda e: e.memset(g["ones_row_f"][:], 1.0))
    b.op("gpsimd", lambda e: e.memset(g["ones_row_b"][:], 1.0))
    g["eps"] = b.sb("g_eps", [1, 1], F32)
    b.op("gpsimd", lambda e: e.memset(g["eps"][:], EPS))
    sg = b.S("glob")
    b.dma("gpsimd", g["ident"][:], ident_d, sem=sg)
    g["tok"] = b.dma("gpsimd", g["onehot"][:], oh_d, sem=sg)
    g["sem"] = sg
    for eng in ("tensor", "scalar", "vector"):
        b.wait(eng, sg, g["tok"])
    return g


def ln_scratch(b, tag, Tmax):
    sc = types.SimpleNamespace()
    sc.xg = b.sb(f"{tag}_xg", [128, DC, Tmax], F32)
    sc.xhat = b.sb(f"{tag}_xhat", [128, DC, Tmax], BF16)
    sc.xsq = b.sb(f"{tag}_xsq", [128, 2, 512], F32)
    sc.mrow = b.sb(f"{tag}_mrow", [1, Tmax], F32)
    sc.vrow = b.sb(f"{tag}_vrow", [1, Tmax], F32)
    sc.trow = b.sb(f"{tag}_trow", [1, Tmax], F32)
    sc.arow = b.sb(f"{tag}_arow", [1, Tmax], F32)
    sc.brow = b.sb(f"{tag}_brow", [1, Tmax], F32)
    sc.absb = b.sb(f"{tag}_absb", [128, 2, Tmax], F32)
    sc.dep_xsq = [0, 0]
    sc.dep_rows = 0
    sc.dep_absb = 0
    sc.dep_xg = []
    sc.dep_xhat = []
    sc.sem_xg = b.S(f"{tag}_sxg")
    return sc


def emit_ln(b, g, sc, x_src, T, sbanks=(0, 1, 2, 3), abanks=(4, 5, 6, 7)):
    NS = nslices(T)
    for (s, v) in sc.dep_xg:
        b.wait("sync", s, v)
    tok = b.dma("sync", sc.xg[:, :, 0:T], fm(x_src), sem=sc.sem_xg)
    b.wait("scalar", sc.sem_xg, tok)
    b.wait("tensor", sc.sem_xg, tok)
    for i in sbanks:
        b.use_bank(i, "tensor")
    b.wait("scalar", b.sdve, getattr(sc, "dep_tmul", 0))
    slot = 0
    for c in range(DC):
        for ni, (no, nw) in enumerate(NS):
            b.op("tensor", lambda e, c=c, no=no, nw=nw, ni=ni: e.matmul(
                b.banks[sbanks[ni]][0:1, 0:nw], g["ones_col"][:], sc.xg[:, c, no:no + nw],
                start=(c == 0), stop=(c == DC - 1)))
            xs = sc.xsq[:, slot % 2, :]
            b.wait("scalar", b.spe, sc.dep_xsq[slot % 2])
            va = b.op("scalar", lambda e, c=c, no=no, nw=nw, xs=xs: e.activation(
                out=xs[:, 0:nw], in_=sc.xg[:, c, no:no + nw], func=AF.Square),
                inc=b.sact)
            b.wait("tensor", b.sact, va)
            v = b.op("tensor", lambda e, c=c, nw=nw, ni=ni, xs=xs: e.matmul(
                b.banks[sbanks[2 + ni]][0:1, 0:nw], g["ones_col"][:],
                xs[:, 0:nw], start=(c == 0), stop=(c == DC - 1)), inc=b.spe)
            sc.dep_xsq[slot % 2] = v
            slot += 1
    stats_done = b.spe.n
    b.wait("vector", b.spe, stats_done)
    b.wait("vector", b.spe, sc.dep_rows)
    def _chain(fn, eng="vector"):
        sem = {"vector": b.sdve, "scalar": b.sact}[eng]
        for s2 in (b.sdve, b.sact):
            b.wait(eng, s2, s2.n)
        return b.op(eng, fn, inc=sem)
    for ni, (no, nw) in enumerate(NS):
        mean = sc.mrow[0:1, no:no + nw]
        var = sc.vrow[0:1, no:no + nw]
        tr = sc.trow[0:1, no:no + nw]
        arow = sc.arow[0:1, no:no + nw]
        brow = sc.brow[0:1, no:no + nw]
        _chain(lambda e, ni=ni, nw=nw, mean=mean: e.tensor_scalar(
            mean, b.banks[sbanks[ni]][0:1, 0:nw], 1.0 / D, None, ALU.mult))
        _chain(lambda e, ni=ni, nw=nw, var=var: e.tensor_scalar(
            var, b.banks[sbanks[2 + ni]][0:1, 0:nw], 1.0 / D, None, ALU.mult))
        _chain(lambda e, mean=mean, tr=tr: e.scalar_tensor_tensor(
            tr, mean, -1.0, mean, ALU.mult, ALU.mult))
        _chain(lambda e, var=var, tr=tr: e.tensor_add(var, var, tr))
        _chain(lambda e, var=var: e.activation(
            out=var, in_=var, func=AF.Ln, bias=g["eps"][0:1, 0:1]), "scalar")
        _chain(lambda e, arow=arow, var=var: e.activation(
            out=arow, in_=var, func=AF.Exp, scale=-0.5), "scalar")
        vv = _chain(lambda e, brow=brow, mean=mean, arow=arow:
                    e.scalar_tensor_tensor(brow, mean, -1.0, arow, ALU.mult, ALU.mult))
    rows_done = b.sdve.n
    for i in sbanks:
        b.release_bank(i, b.sdve, rows_done)
    b.wait("tensor", b.sdve, rows_done)
    bc_done = None
    for ni, (no, nw) in enumerate(NS):
        b.use_bank(abanks[ni], "tensor")
        b.use_bank(abanks[2 + ni], "tensor")
        b.op("tensor", lambda e, ni=ni, no=no, nw=nw: e.matmul(
            b.banks[abanks[ni]][:, 0:nw], g["ones_row_f"][0:1, 0:128],
            sc.arow[0:1, no:no + nw], start=True, stop=True))
        bc_done = b.op("tensor", lambda e, ni=ni, no=no, nw=nw: e.matmul(
            b.banks[abanks[2 + ni]][:, 0:nw], g["ones_row_f"][0:1, 0:128],
            sc.brow[0:1, no:no + nw], start=True, stop=True), inc=b.spe)
    sc.dep_rows = bc_done
    b.wait("scalar", b.spe, bc_done)
    b.wait("scalar", b.sdve, sc.dep_absb)
    for ni, (no, nw) in enumerate(NS):
        b.op("scalar", lambda e, ni=ni, no=no, nw=nw: e.copy(
            out=sc.absb[:, 0, no:no + nw], in_=b.banks[abanks[ni]][:, 0:nw]))
        va = b.op("scalar", lambda e, ni=ni, no=no, nw=nw: e.copy(
            out=sc.absb[:, 1, no:no + nw], in_=b.banks[abanks[2 + ni]][:, 0:nw]),
            inc=b.sact)
    ab_done = b.sact.n
    for ni in range(4):
        b.release_bank(abanks[ni], b.sact, ab_done)
    b.wait("vector", b.sact, ab_done)
    for (s, v) in sc.dep_xhat:
        b.wait("vector", s, v)
    sc.dep_xhat = []
    tmul = sc.xsq[:].rearrange("p a b -> p (a b)")
    for c in range(DC):
        b.op("vector", lambda e, c=c, T=T, tmul=tmul: e.tensor_mul(
            tmul[:, 0:T], sc.xg[:, c, 0:T], sc.absb[:, 0, 0:T]))
        v = b.op("vector", lambda e, c=c, T=T, tmul=tmul: e.tensor_add(
            sc.xhat[:, c, 0:T], tmul[:, 0:T], sc.absb[:, 1, 0:T]), inc=b.sdve)
    xhat_done = b.sdve.n
    sc.dep_absb = xhat_done
    sc.dep_tmul = xhat_done
    sc.dep_xg = [(b.spe, stats_done), (b.sdve, xhat_done)]
    return {"xhat_dve": xhat_done, "tok": tok, "stats_done": stats_done}



def emit_block0(b, g, aps):
    sc = ln_scratch(b, "b0", 1024)
    wk = b.sb("b0_wk", [128, DC, D], BF16)
    wv = b.sb("b0_wv", [128, DC, D], BF16)
    wq = b.sb("b0_wq", [128, DC, D], BF16)
    wo = b.sb("b0_wo", [128, DC, D], BF16)
    brows = b.sb("b0_brows", [1, 3 * D], BF16)
    bo_pc = b.sb("b0_bo", [128, DC], F32)
    ktm = b.sb("b0_ktm", [128, D], BF16)
    vtm = b.sb("b0_vtm", [128, D], BF16)
    kvsb = b.sb("b0_kvsb", [128, 2, 512], BF16)
    qfm = b.sb("b0_qfm", [128, DC, 1024], BF16)
    t1 = b.sb("b0_t1", [128, 512], F32)
    e1 = b.sb("b0_e1", [128, 512], F32)

    sw = b.S("b0w")
    sbo = b.S("b0bo")
    for dst, src in ((wk, aps["wk"]), (wv, aps["wv"]), (wq, aps["wq"]), (wo, aps["wo"])):
        b.dma("gpsimd", dst[:], src.rearrange("(k p) o -> p k o", p=128), sem=sw)
    b.dma("gpsimd", brows[0:1, 0:D], aps["bq_r"], sem=sw)
    b.dma("gpsimd", brows[0:1, D:2 * D], aps["bk_r"], sem=sw)
    wtok = b.dma("gpsimd", brows[0:1, 2 * D:3 * D], aps["bv_r"], sem=sw)
    botok = b.dma("sync", bo_pc[:], aps["bo_pc"], sem=sbo)
    b.wait("tensor", sw, wtok)

    import os
    _nokv = os.environ.get("B0_NOKV", "0") == "1"
    _kvp = os.environ.get("B0_KV_PARTS", "full")
    KVG = [] if _nokv else groups_of(SEQ, 1024)
    ntiles_total = sum(T // 128 for _, T in KVG)
    kv_started = False
    kv_done = 0
    tcount = 0
    for gi, (off, T) in enumerate(KVG):
        d = emit_ln(b, g, sc, aps["xkv"][:, off:off + T], T,
                    sbanks=(2, 3, 4, 5), abanks=(6, 7, 2, 3))
        b.wait("tensor", b.sdve, d["xhat_dve"])
        for tile in range(T // 128):
            dep_k = 0
            dep_v = 0
            for which in ("k", "v"):
                for hf in range(2):
                    ib = 4 + (0 if which == "k" else 2) + hf
                    bank = b.use_bank(ib, "tensor")
                    wsb = wk if which == "k" else wv
                    brow_off = D if which == "k" else 2 * D
                    for kc in range(DC):
                        b.op("tensor", lambda e, kc=kc, tile=tile, hf=hf, wsb=wsb, bank=bank: e.matmul(
                            bank[:], sc.xhat[:, kc, tile * 128:(tile + 1) * 128],
                            wsb[:, kc, hf * 512:(hf + 1) * 512],
                            start=(kc == 0), stop=False))
                    v = b.op("tensor", lambda e, hf=hf, brow_off=brow_off, bank=bank: e.matmul(
                        bank[:], g["ones_row_b"][0:1, 0:128],
                        brows[0:1, brow_off + hf * 512:brow_off + (hf + 1) * 512],
                        start=False, stop=True), inc=b.spe)
                    if which == "k":
                        b.wait("vector", b.spe, v)
                        vd = b.op("vector", lambda e, bank=bank: e.tensor_single_scalar(
                            t1[:], bank[:], 0.0, ALU.min), inc=b.sdve)
                        b.wait("scalar", b.sdve, vd)
                        va = b.op("scalar", lambda e: e.activation(
                            out=e1[:], in_=t1[:], func=AF.Exp), inc=b.sact)
                        b.wait("vector", b.sact, va)
                        dep_k = b.op("vector", lambda e, hf=hf, bank=bank: e.scalar_tensor_tensor(
                            ktm[:, hf * 512:(hf + 1) * 512], bank[:], 0.0, e1[:],
                            ALU.max, ALU.add), inc=b.sdve)
                        b.release_bank(ib, b.sdve, dep_k)
                    else:
                        b.wait("scalar", b.spe, v)
                        dep_v = b.op("scalar", lambda e, hf=hf, bank=bank: e.copy(
                            out=vtm[:, hf * 512:(hf + 1) * 512], in_=bank[:]), inc=b.sact)
                        b.release_bank(ib, b.sact, dep_v)
            b.wait("tensor", b.sdve, dep_k)
            b.wait("tensor", b.sact, dep_v)
            sc.dep_xhat = [(b.spe, b.spe.n)]
            tcount += 1
            last = (tcount == ntiles_total)
            if _kvp == "proj":
                continue
            for h in range(16):
                v = b.op("tensor", lambda e, h=h, ks=kv_started, last=last: e.matmul(
                    b.banks[h // 8][0:64, (h % 8) * 64:(h % 8 + 1) * 64],
                    ktm[:, h * 64:(h + 1) * 64], vtm[:, h * 64:(h + 1) * 64],
                    start=((not ks) and h % 8 == 0), stop=last, skip_group_check=True),
                    inc=(b.spe if (last and h == 15) else None))
                if v:
                    kv_done = v
            kv_started = True
    if _kvp == "proj":
        _nokv = True
    if _nokv or _kvp == "kv":
        pass
    if not _nokv and _kvp not in ("kv",):
        b.wait("scalar", b.spe, kv_done)
        for i in range(2):
            va = b.op("scalar", lambda e, i=i: e.copy(
                out=kvsb[0:64, i, :], in_=b.banks[i][0:64, 0:512]), inc=b.sact)
            b.release_bank(i, b.sact, va)
        kv_sb_done = b.sact.n
        if _kvp == "full":
            b.wait("sync", b.sact, kv_sb_done)
            kv_rep = b.dma("sync", kvsb[64:128, :, :], kvsb[0:64, :, :], sem=b.S("kvrep"))
        else:
            kv_rep = 0
    else:
        kv_rep = 0

    import os
    _stop = os.environ.get("B0_STOP", "full")
    MG = groups_of(NT, 1024, first=HALO)
    st_tok = None
    for gi, (off, T) in enumerate(MG):
        d = emit_ln(b, g, sc, aps["xmain"][:, off:off + T], T,
                    sbanks=(0, 1, 2, 3), abanks=(4, 5, 6, 7))
        NS = nslices(T)
        b.wait("tensor", b.sdve, d["xhat_dve"])
        if _stop == "ln":
            st_tok = b.dma("sync", fm(aps["xres_out"][:, off:off + T]), sc.xg[:, :, 0:T],
                           sem=b.S("b0st"))
            sc.dep_xg += [(b.S("b0st"), st_tok)]
            continue
        for oc in range(DC):
            for ni, (no, nw) in enumerate(NS):
                ib = (oc * len(NS) + ni) % 4
                bank = b.use_bank(ib, "tensor")
                for kc in range(DC):
                    b.op("tensor", lambda e, kc=kc, oc=oc, no=no, nw=nw, bank=bank: e.matmul(
                        bank[:, 0:nw], wq[:, kc, oc * 128:(oc + 1) * 128],
                        sc.xhat[:, kc, no:no + nw], start=(kc == 0), stop=False))
                v = b.op("tensor", lambda e, oc=oc, nw=nw, bank=bank: e.matmul(
                    bank[:, 0:nw], brows[0:1, oc * 128:(oc + 1) * 128],
                    g["ones_row_b"][0:1, 0:nw], start=False, stop=True), inc=b.spe)
                b.wait("vector", b.spe, v)
                vd = b.op("vector", lambda e, nw=nw, bank=bank: e.tensor_single_scalar(
                    t1[:, 0:nw], bank[:, 0:nw], 0.0, ALU.min), inc=b.sdve)
                b.wait("scalar", b.sdve, vd)
                va = b.op("scalar", lambda e, nw=nw: e.activation(
                    out=e1[:, 0:nw], in_=t1[:, 0:nw], func=AF.Exp), inc=b.sact)
                b.wait("vector", b.sact, va)
                vd = b.op("vector", lambda e, oc=oc, no=no, nw=nw, bank=bank: e.scalar_tensor_tensor(
                    qfm[:, oc, no:no + nw], bank[:, 0:nw], 0.0, e1[:, 0:nw],
                    ALU.max, ALU.add), inc=b.sdve)
                b.release_bank(ib, b.sdve, vd)
        q_done = b.sdve.n
        if _stop == "q":
            b.wait("sync", b.sdve, q_done)
            st_tok = b.dma("sync", fm(aps["xres_out"][:, off:off + T]), sc.xg[:, :, 0:T],
                           sem=b.S("b0st"))
            sc.dep_xg += [(b.S("b0st"), st_tok)]
            sc.dep_xhat = [(b.spe, b.spe.n)]
            continue
        ao = sc.xhat
        b.wait("tensor", b.sdve, q_done)
        b.wait("tensor", b.S("kvrep"), kv_rep)
        for hp in range(DC):
            for ni, (no, nw) in enumerate(NS):
                ib = 4 + (hp * len(NS) + ni) % 4
                bank = b.use_bank(ib, "tensor")
                h0, h1 = 2 * hp, 2 * hp + 1
                b.op("tensor", lambda e, h0=h0, hp=hp, no=no, nw=nw, bank=bank: e.matmul(
                    bank[0:64, 0:nw], kvsb[0:64, h0 // 8, (h0 % 8) * 64:((h0 % 8) + 1) * 64],
                    qfm[0:64, hp, no:no + nw], start=True, stop=True))
                v = b.op("tensor", lambda e, h1=h1, hp=hp, no=no, nw=nw, bank=bank: e.matmul(
                    bank[64:128, 0:nw], kvsb[64:128, h1 // 8, (h1 % 8) * 64:((h1 % 8) + 1) * 64],
                    qfm[64:128, hp, no:no + nw], start=True, stop=True), inc=b.spe)
                b.wait("scalar", b.spe, v)
                va = b.op("scalar", lambda e, hp=hp, no=no, nw=nw, bank=bank: e.copy(
                    out=ao[:, hp, no:no + nw], in_=bank[:, 0:nw]), inc=b.sact)
                b.release_bank(ib, b.sact, va)
        ao_done = b.sact.n
        if _stop == "attn":
            b.wait("sync", b.sact, ao_done)
            st_tok = b.dma("sync", fm(aps["xres_out"][:, off:off + T]), sc.xg[:, :, 0:T],
                           sem=b.S("b0st"))
            sc.dep_xg += [(b.S("b0st"), st_tok)]
            sc.dep_xhat = [(b.sact, ao_done), (b.spe, b.spe.n)]
            continue
        b.wait("tensor", b.sact, ao_done)
        b.wait("vector", b.S("b0bo"), botok)
        for oc in range(DC):
            for ni, (no, nw) in enumerate(NS):
                ib = (oc * len(NS) + ni) % 4
                bank = b.use_bank(ib, "tensor")
                for kc in range(DC):
                    b.op("tensor", lambda e, kc=kc, oc=oc, no=no, nw=nw, bank=bank: e.matmul(
                        bank[:, 0:nw], wo[:, kc, oc * 128:(oc + 1) * 128],
                        ao[:, kc, no:no + nw], start=(kc == 0), stop=(kc == DC - 1)),
                    inc=(b.spe if kc == DC - 1 else None))
                v = b.spe.n
                b.wait("vector", b.spe, v)
                vd = b.op("vector", lambda e, oc=oc, no=no, nw=nw, bank=bank: e.scalar_tensor_tensor(
                    sc.xg[:, oc, no:no + nw], bank[:, 0:nw], bo_pc[:, oc:oc + 1],
                    sc.xg[:, oc, no:no + nw], ALU.add, ALU.add), inc=b.sdve)
                b.release_bank(ib, b.sdve, vd)
        res_done = b.sdve.n
        b.wait("sync", b.sdve, res_done)
        st_tok = b.dma("sync", fm(aps["xres_out"][:, off:off + T]), sc.xg[:, :, 0:T],
                       sem=b.S("b0st"))
        sc.dep_xg += [(b.S("b0st"), st_tok), (b.sdve, res_done)]
        sc.dep_xhat = [(b.spe, b.spe.n)]
    return {"out_store": st_tok}



def emit_ffn(b, g, aps, cfg):
    tag = cfg["tag"]
    geglu = cfg["geglu"]
    NTb = cfg["NTb"]
    groups = cfg["groups"]
    top2 = cfg["routing"] == "top2"
    has_bias = cfg["has_bias"]
    import os
    dbg = aps.get("dbg")

    gate_sb = b.sb(f"{tag}_gate", [128, DC, 8], BF16)
    combine_fm = b.sb(f"{tag}_cmbfm", [8, NTb], BF16)
    w1h = b.sb(f"{tag}_w1h", [128, DC, 2048], BF16)
    w2h = b.sb(f"{tag}_w2h", [128, DC, 2048], BF16) if geglu else None
    wyh = b.sb(f"{tag}_wyh", [128, 16, 1024], BF16)
    if has_bias:
        b1_pc = b.sb(f"{tag}_b1pc", [128, 8 * 32], F32)
        b2_pc = b.sb(f"{tag}_b2pc", [128, 8 * 8], F32)
        sb1_pc = b.sb(f"{tag}_sb1pc", [128, 32], F32)
        sb2_pc = b.sb(f"{tag}_sb2pc", [128, 8], F32)
    cb = b.sb(f"{tag}_cb", [128, 1024], F32)

    def alloc_ln():
        sc = ln_scratch(b, tag, 1024)
        sc.lb_scr = [b.sb(f"{tag}_r{i}", [128, 8], F32) for i in range(4)]
        sc.m1 = b.sb(f"{tag}_m1", [128, 1], F32)
        sc.m2 = b.sb(f"{tag}_m2", [128, 1], F32)
        sc.den = b.sb(f"{tag}_den", [128, 1], F32)
        return sc

    passbufs = {}

    def alloc_pass():
        passbufs["xhat"] = [b.sb(f"{tag}_xh{i}", [128, DC, 1024], BF16) for i in range(2)]
        passbufs["hp"] = b.sb(f"{tag}_hp", [128, 16, 1024], BF16)
        passbufs["gt"] = [b.sb(f"{tag}_gt{i}", [128, 1024], BF16) for i in range(2)]
        passbufs["stage"] = [b.sb(f"{tag}_st{i}", [128, 1024], F32) for i in range(2)]
        passbufs["xgd"] = [b.sb(f"{tag}_xgd{i}", [128, 1024], F32) for i in range(2)]
        return passbufs

    sc, _ = b.overlay(alloc_ln, alloc_pass)

    sprog = b.S(f"{tag}_prog")
    spros = b.S(f"{tag}_pros")
    gtok = b.dma("gpsimd", gate_sb[:], aps["gate"].rearrange("(k p) r -> p k r", p=128), sem=sprog)
    stok = 0
    if has_bias:
        b.dma("sync", b1_pc[:], aps["b1_pc"], sem=spros)
        b.dma("sync", b2_pc[:], aps["b2_pc"], sem=spros)
        b.dma("sync", sb1_pc[:], aps["sb1_pc"], sem=spros)
        stok = b.dma("sync", sb2_pc[:], aps["sb2_pc"], sem=spros)
    for eng in ("scalar", "vector", "tensor"):
        b.wait(eng, sprog, gtok)
        b.wait(eng, spros, stok)

    for gi, (off, T) in enumerate(groups):
        d = emit_ln(b, g, sc, aps["xres_in"][:, off:off + T], T,
                    sbanks=(0, 1, 2, 3), abanks=(4, 5, 6, 7))
        b.wait("sync", b.sdve, d["xhat_dve"])
        sttok = b.dma("sync", fm(aps["xhat_hbm"][:, off:off + T]), sc.xhat[:, :, 0:T],
                      sem=b.S(f"{tag}_xst"))
        sc.dep_xg += [(b.S(f"{tag}_xst"), sttok)]
        b.wait("tensor", b.sdve, d["xhat_dve"])
        b.wait("tensor", g["sem"], g["tok"])
        pu, eq, pm, w = sc.lb_scr
        tv = 0
        for tile in range(T // 128):
            lb = b.use_bank(0, "tensor")
            for kc in range(DC):
                b.op("tensor", lambda e, kc=kc, tile=tile, lb=lb: e.matmul(
                    lb[:, 0:8], sc.xhat[:, kc, tile * 128:(tile + 1) * 128],
                    gate_sb[:, kc, :], start=(kc == 0), stop=(kc == DC - 1)),
                    inc=(b.spe if kc == DC - 1 else None))
            lv = b.spe.n
            b.wait("vector", b.spe, lv)
            def chain(fn, eng="vector"):
                sem = {"vector": b.sdve, "scalar": b.sact}[eng]
                for s2 in (b.sdve, b.sact):
                    b.wait(eng, s2, s2.n)
                return b.op(eng, fn, inc=sem)
            if top2:
                chain(lambda e, lb=lb: e.reduce_max(sc.m1[:], lb[:, 0:8], axis=AX.X))
                chain(lambda e: e.tensor_single_scalar(sc.m1[:], sc.m1[:], -1.0, ALU.mult))
                chain(lambda e, lb=lb: e.activation(
                    out=pu[:], in_=lb[:, 0:8], func=AF.Exp, bias=sc.m1[:]), "scalar")
                chain(lambda e: e.reduce_max(sc.m1[:], pu[:], axis=AX.X))
                chain(lambda e: e.tensor_single_scalar(eq[:], pu[:], sc.m1[:], ALU.is_ge))
                chain(lambda e: e.scalar_tensor_tensor(pm[:], eq[:], -1e30, pu[:], ALU.mult, ALU.add))
                chain(lambda e: e.reduce_max(sc.m2[:], pm[:], axis=AX.X))
                chain(lambda e: e.tensor_single_scalar(eq[:], pu[:], sc.m2[:], ALU.is_ge))
                chain(lambda e: e.tensor_mul(w[:], pu[:], eq[:]))
                chain(lambda e: e.tensor_add(sc.den[:], sc.m1[:], sc.m2[:]))
                chain(lambda e: e.reciprocal(sc.m1[:], sc.den[:]))
                vd = chain(lambda e: e.tensor_single_scalar(pm[:], w[:], sc.m1[:], ALU.mult))
            else:
                chain(lambda e, lb=lb: e.reduce_max(sc.m1[:], lb[:, 0:8], axis=AX.X))
                vd = chain(lambda e, lb=lb: e.tensor_single_scalar(
                    pm[:], lb[:, 0:8], sc.m1[:], ALU.is_ge))
            b.release_bank(0, b.sdve, vd)
            trb = b.use_bank(1, "tensor")
            b.wait("tensor", b.sdve, vd)
            tv = b.op("tensor", lambda e, trb=trb: e.matmul(
                trb[0:8, 0:128], pm[:], g["ident"][:], is_transpose=True,
                start=True, stop=True, skip_group_check=True), inc=b.spe)
            b.wait("scalar", b.spe, tv)
            va = b.op("scalar", lambda e, off=off, tile=tile, trb=trb: e.copy(
                out=combine_fm[0:8, off + tile * 128:off + (tile + 1) * 128],
                in_=trb[0:8, 0:128]), inc=b.sact)
            b.release_bank(1, b.sact, va)
            b.wait("vector", b.spe, tv)
        sc.dep_xhat = [(b.spe, b.spe.n), (b.S(f"{tag}_xst"), sttok)]
    if os.environ.get("FFN_CMB_CONST"):
        b.wait("gpsimd", b.sact, b.sact.n)
        b.wait("gpsimd", b.sdve, b.sdve.n)
        b.wait("gpsimd", b.spe, b.spe.n)
        b.op("gpsimd", lambda e: e.memset(combine_fm[0:8, 0:NTb], 0.125))
        cwtok = b.dma("gpsimd", g["ident"][0:1, 0:1], g["ident"][0:1, 0:1], sem=b.S(f"{tag}_cmc"))
        for eng in ("tensor", "scalar", "vector"):
            b.wait(eng, b.S(f"{tag}_cmc"), cwtok)
    if os.environ.get("FFN_DUMP_CMB_EARLY") and dbg is not None:
        b.wait("gpsimd", b.sact, b.sact.n)
        b.dma("gpsimd", dbg[0:8, 0:NTb], combine_fm[0:8, 0:NTb],
              sem=b.S(f"{tag}_dmp2"))
    b.phase_gate()

    xhat_sb = passbufs["xhat"]
    hp = passbufs["hp"]
    gt = passbufs["gt"]
    stage = passbufs["stage"]
    xgd = passbufs["xgd"]
    dep_w = [0, 0, 0]
    dep_xh = [0, 0]
    dep_stage = [0, 0]
    dep_xgd = [0, 0]
    dep_cb = 0
    shared_tok = 0
    import os
    npass = int(os.environ.get("FFN_PASSES", "9"))
    glim = int(os.environ.get("FFN_GROUPS_LIMIT", "0"))
    if glim:
        groups = groups[:glim]
    dump_hp = os.environ.get("FFN_DUMP_HP", "0") == "1"
    dump_xh = os.environ.get("FFN_DUMP_XH", "0") == "1"
    dump_w1 = os.environ.get("FFN_DUMP_W1", "0") == "1"
    for pi in range(npass):
        is_sh = pi == 0
        e = pi - 1
        for half in (0, 1):
            hs = half * 2048
            if is_sh:
                w1src = aps["sw1"][:, hs:hs + 2048]
                wysrc = aps["swy"][hs:hs + 2048, :]
                w2src = aps["sw2"][:, hs:hs + 2048] if geglu else None
            else:
                w1src = aps["w1"][e, :, hs:hs + 2048]
                wysrc = aps["wy"][e, hs:hs + 2048, :]
                w2src = aps["w2"][e, :, hs:hs + 2048] if geglu else None
            sw = b.S(f"{tag}_w")
            b.wait("gpsimd", b.spe, dep_w[0])
            wt = b.dma("gpsimd", w1h[:], w1src.rearrange("(k p) m -> p k m", p=128), sem=sw)
            if geglu:
                b.wait("gpsimd", b.spe, dep_w[1])
                wt = b.dma("gpsimd", w2h[:], w2src.rearrange("(k p) m -> p k m", p=128), sem=sw)
            b.wait("gpsimd", b.spe, dep_w[2])
            wt = b.dma("gpsimd", wyh[:], wysrc.rearrange("(m p) d -> p m d", p=128), sem=sw)
            b.wait("tensor", sw, wt)
            if dump_w1:
                b.wait("gpsimd", sw, wt)
                for cc_ in range(8):
                    b.dma("gpsimd", aps["out_hbm"][cc_ * 128:(cc_ + 1) * 128, 0:2048],
                          w1h[:, cc_, :])
                break
            for gi, (off, T) in enumerate(groups):
                NS = nslices(T)
                xh = xhat_sb[gi % 2]
                sxh = b.S(f"{tag}_xh{gi % 2}")
                b.wait("sync", b.spe, dep_xh[gi % 2])
                xtok = b.dma("sync", xh[:, :, 0:T], fm(aps["xhat_hbm"][:, off:off + T]), sem=sxh)
                b.wait("tensor", sxh, xtok)
                if dump_xh:
                    b.wait("gpsimd", sxh, xtok)
                    for cc_ in range(8):
                        b.dma("gpsimd", aps["out_hbm"][cc_ * 128:(cc_ + 1) * 128, off:off + T],
                              xh[:, cc_, 0:T])
                    continue
                if not is_sh:
                    b.wait("scalar", b.sdve, dep_cb)
                    for ni, (no, nw) in enumerate(NS):
                        bank = b.use_bank(4 + ni, "tensor")
                        mv = b.op("tensor", lambda e_, e=e, off=off, no=no, nw=nw, bank=bank: e_.matmul(
                            bank[:, 0:nw], g["onehot"][0:8, e * 128:(e + 1) * 128],
                            combine_fm[0:8, off + no:off + no + nw],
                            start=True, stop=True), inc=b.spe)
                        b.wait("scalar", b.spe, mv)
                        va = b.op("scalar", lambda e_, no=no, nw=nw, bank=bank: e_.copy(
                            out=cb[:, no:no + nw], in_=bank[:, 0:nw]), inc=b.sact)
                        b.release_bank(4 + ni, b.sact, va)
                    cb_ready = b.sact.n
                    b.wait("vector", b.sact, cb_ready)
                    if os.environ.get("FFN_DUMP_CB") and pi == 1 and dbg is not None:
                        b.wait("gpsimd", b.sact, cb_ready)
                        b.dma("gpsimd", dbg[16:144, off:off + T],
                              cb[:, 0:T], sem=b.S(f"{tag}_dmp"))
                        dtok = b.dma("gpsimd", dbg[8:16, off:off + T],
                                     combine_fm[0:8, off:off + T], sem=b.S(f"{tag}_dmp"))
                        b.wait("scalar", b.S(f"{tag}_dmp"), dtok)
                dep_gt = [0, 0]
                mlim = int(os.environ.get("FFN_M_LIMIT", "16"))
                for m in range(mlim):
                    pair = 2 * (m % 2)
                    hv = None
                    va_m = None
                    for ni, (no, nw) in enumerate(NS):
                        bank = b.use_bank(pair + ni, "tensor")
                        for kc in range(DC):
                            b.op("tensor", lambda e_, kc=kc, m=m, no=no, nw=nw, bank=bank, xh=xh: e_.matmul(
                                bank[:, 0:nw], w1h[:, kc, m * 128:(m + 1) * 128],
                                xh[:, kc, no:no + nw], start=(kc == 0), stop=(kc == DC - 1)),
                                inc=(b.spe if kc == DC - 1 else None))
                        hv = b.spe.n
                        b.wait("scalar", b.spe, hv)
                        b.wait("scalar", b.sdve, dep_gt[m % 2])
                        if has_bias:
                            bias = (sb1_pc[:, half * 16 + m:half * 16 + m + 1] if is_sh
                                    else b1_pc[:, e * 32 + half * 16 + m:e * 32 + half * 16 + m + 1])
                        else:
                            bias = 0.0
                        dst = (gt[m % 2][:, no:no + nw] if geglu else hp[:, m, no:no + nw])
                        va_m = b.op("scalar", lambda e_, dst=dst, bank=bank, bias=bias, nw=nw: e_.activation(
                            out=dst, in_=bank[:, 0:nw], func=AF.Gelu, bias=bias), inc=b.sact)
                        b.release_bank(pair + ni, b.sact, va_m)
                    if geglu:
                        for ni, (no, nw) in enumerate(NS):
                            ib = 4 + 2 * (m % 2) + ni
                            ubank = b.use_bank(ib, "tensor")
                            for kc in range(DC):
                                b.op("tensor", lambda e_, kc=kc, m=m, no=no, nw=nw, ubank=ubank, xh=xh: e_.matmul(
                                    ubank[:, 0:nw], w2h[:, kc, m * 128:(m + 1) * 128],
                                    xh[:, kc, no:no + nw], start=(kc == 0), stop=(kc == DC - 1)),
                                    inc=(b.spe if kc == DC - 1 else None))
                            uv = b.spe.n
                            b.wait("vector", b.spe, uv)
                            b.wait("vector", b.sact, va_m)
                            vd = b.op("vector", lambda e_, m=m, no=no, nw=nw, ubank=ubank: e_.tensor_tensor(
                                hp[:, m, no:no + nw], ubank[:, 0:nw],
                                gt[m % 2][:, no:no + nw], ALU.mult), inc=b.sdve)
                            b.release_bank(ib, b.sdve, vd)
                            dep_gt[m % 2] = vd
                hp_done_dve = b.sdve.n
                hp_done_act = b.sact.n
                if dump_hp:
                    b.wait("gpsimd", b.sdve, hp_done_dve)
                    b.wait("gpsimd", b.sact, hp_done_act)
                    for mm_ in range(8):
                        b.dma("gpsimd", aps["out_hbm"][mm_ * 128:(mm_ + 1) * 128, off:off + T],
                              hp[:, mm_, 0:T])
                    continue
                if geglu:
                    b.wait("tensor", b.sdve, hp_done_dve)
                else:
                    b.wait("tensor", b.sact, hp_done_act)
                for dd in range(8):
                    if is_sh and half == 0:
                        slot2 = dd % 2
                        b.wait("sync", b.sdve, dep_xgd[slot2])
                        gtok = b.dma("sync", xgd[slot2][:, 0:T],
                                     aps["xres_in"][dd * 128:(dd + 1) * 128, off:off + T],
                                     sem=b.S(f"{tag}_xgd{slot2}"))
                        b.wait("vector", b.S(f"{tag}_xgd{slot2}"), gtok)
                    pair = 2 * (dd % 2)
                    for ni, (no, nw) in enumerate(NS):
                        bank = b.use_bank(pair + ni, "tensor")
                        for m in range(16):
                            b.op("tensor", lambda e_, m=m, dd=dd, no=no, nw=nw, bank=bank: e_.matmul(
                                bank[:, 0:nw], wyh[:, m, dd * 128:(dd + 1) * 128],
                                hp[:, m, no:no + nw], start=(m == 0), stop=(m == 15)),
                                inc=(b.spe if m == 15 else None))
                        yv = b.spe.n
                        slot = (dd * len(NS) + ni) % 2
                        stg = stage[slot][:, 0:nw]
                        b.wait("vector", b.S(f"{tag}_st{slot}"), dep_stage[slot])
                        b.wait("vector", b.spe, yv)
                        if is_sh and half == 0:
                            vd = b.op("vector", lambda e_, stg=stg, bank=bank, dd=dd, no=no, nw=nw: e_.tensor_add(
                                stg, bank[:, 0:nw], xgd[dd % 2][:, no:no + nw]), inc=b.sdve)
                            dep_xgd[dd % 2] = vd
                        elif is_sh:
                            if has_bias:
                                vd = b.op("vector", lambda e_, stg=stg, bank=bank, dd=dd, nw=nw: e_.tensor_scalar(
                                    stg, bank[:, 0:nw], sb2_pc[:, dd:dd + 1], None, ALU.add), inc=b.sdve)
                            else:
                                vd = b.op("vector", lambda e_, stg=stg, bank=bank, nw=nw: e_.tensor_copy(
                                    stg, bank[:, 0:nw]), inc=b.sdve)
                        else:
                            if os.environ.get("FFN_NO_CB"):
                                vd = b.op("vector", lambda e_, stg=stg, bank=bank, nw=nw: e_.tensor_single_scalar(
                                    stg, bank[:, 0:nw], 0.1, ALU.mult), inc=b.sdve)
                            elif has_bias and half == 1:
                                vd = b.op("vector", lambda e_, stg=stg, bank=bank, dd=dd, e=e, no=no, nw=nw: e_.scalar_tensor_tensor(
                                    stg, bank[:, 0:nw], b2_pc[:, e * 8 + dd:e * 8 + dd + 1],
                                    cb[:, no:no + nw], ALU.add, ALU.mult), inc=b.sdve)
                            else:
                                vd = b.op("vector", lambda e_, stg=stg, bank=bank, no=no, nw=nw: e_.tensor_tensor(
                                    stg, bank[:, 0:nw], cb[:, no:no + nw], ALU.mult), inc=b.sdve)
                            dep_cb = vd
                        b.release_bank(pair + ni, b.sdve, vd)
                        b.wait("gpsimd", b.sdve, vd)
                        dst = aps["out_hbm"][dd * 128:(dd + 1) * 128, off + no:off + no + nw]
                        if is_sh and half == 0:
                            tok = b.dma("gpsimd", dst, stg, sem=b.S(f"{tag}_st{slot}"))
                        else:
                            tok = b.dma("gpsimd", dst, stg, accum_op=ALU.add,
                                        sem=b.S(f"{tag}_st{slot}"))
                        dep_stage[slot] = tok
                dep_xh[gi % 2] = b.spe.n
            dep_w = [b.spe.n] * 3
    return {"out_tok": b.qgp.n}


def build_ffn(which):
    b = B()
    nc = b.nc
    aps = {}
    if which == "b1":
        NTb = NT
        aps["xres_in"] = nc.dram_tensor("xres_in", [D, NTb], F32, kind="ExternalInput").ap()
        aps["gate"] = nc.dram_tensor("gate", [D, 8], F32, kind="ExternalInput").ap()
        aps["w1"] = nc.dram_tensor("w1", [8, 1024, 4096], F32, kind="ExternalInput").ap()
        aps["wy"] = nc.dram_tensor("wy", [8, 4096, 1024], F32, kind="ExternalInput").ap()
        aps["sw1"] = nc.dram_tensor("sw1", [1024, 4096], F32, kind="ExternalInput").ap()
        aps["swy"] = nc.dram_tensor("swy", [4096, 1024], F32, kind="ExternalInput").ap()
        aps["b1_pc"] = nc.dram_tensor("b1_pc", [128, 256], F32, kind="ExternalInput").ap()
        aps["b2_pc"] = nc.dram_tensor("b2_pc", [128, 64], F32, kind="ExternalInput").ap()
        aps["sb1_pc"] = nc.dram_tensor("sb1_pc", [128, 32], F32, kind="ExternalInput").ap()
        aps["sb2_pc"] = nc.dram_tensor("sb2_pc", [128, 8], F32, kind="ExternalInput").ap()
        aps["out_hbm"] = nc.dram_tensor("out_hbm", [D, NTb], F32, kind="ExternalOutput").ap()
        import os as _os
        _xk = "ExternalOutput" if _os.environ.get("FFN_XHAT_EXT") else "Internal"
        aps["xhat_hbm"] = nc.dram_tensor("xhat_s", [D, NTb], BF16, kind=_xk).ap()
        if _os.environ.get("FFN_DBG"):
            aps["dbg"] = nc.dram_tensor("dbg", [144, NTb], F32, kind="ExternalOutput").ap()
        cfg = {"tag": "f0", "geglu": False, "NTb": NTb, "routing": "top2",
               "has_bias": True, "groups": groups_of(NTb, 1024, first=HALO)}
    else:
        NTb = OWN
        aps["xres_in"] = nc.dram_tensor("xres_in", [D, NTb], F32, kind="ExternalInput").ap()
        aps["gate"] = nc.dram_tensor("gate", [D, 8], F32, kind="ExternalInput").ap()
        aps["w1"] = nc.dram_tensor("w1", [8, 1024, 4096], F32, kind="ExternalInput").ap()
        aps["w2"] = nc.dram_tensor("w2", [8, 1024, 4096], F32, kind="ExternalInput").ap()
        aps["wy"] = nc.dram_tensor("wy", [8, 4096, 1024], F32, kind="ExternalInput").ap()
        aps["sw1"] = nc.dram_tensor("sw1", [1024, 4096], F32, kind="ExternalInput").ap()
        aps["sw2"] = nc.dram_tensor("sw2", [1024, 4096], F32, kind="ExternalInput").ap()
        aps["swy"] = nc.dram_tensor("swy", [4096, 1024], F32, kind="ExternalInput").ap()
        aps["out_hbm"] = nc.dram_tensor("out_hbm", [D, NTb], F32, kind="ExternalOutput").ap()
        aps["xhat_hbm"] = nc.dram_tensor("xhat_s", [D, NTb], BF16, kind="Internal").ap()
        cfg = {"tag": "f1", "geglu": True, "NTb": NTb, "routing": "top1",
               "has_bias": False, "groups": groups_of(NTb, 1024)}
    g = emit_globals(b)
    emit_ffn(b, g, aps, cfg)
    b.final_dma_waits("sync")
    b.final_dma_waits("gpsimd")
    return b.finalize()


def prep_ffn(inputs, core, which):
    pc = lambda v, r: np.ascontiguousarray(
        np.asarray(v, np.float32).reshape(r, -1, 128).transpose(2, 0, 1).reshape(128, -1))
    if which == "b1":
        return {
            "gate": np.ascontiguousarray(np.asarray(inputs["L0_ffn_gate_w"]).T),
            "w1": np.ascontiguousarray(np.asarray(inputs["L0_ffn_W1"], np.float32)),
            "wy": np.ascontiguousarray(np.asarray(inputs["L0_ffn_W2"], np.float32)),
            "sw1": np.ascontiguousarray(np.asarray(inputs["L0_ffn_sW1"], np.float32)),
            "swy": np.ascontiguousarray(np.asarray(inputs["L0_ffn_sW2"], np.float32)),
            "b1_pc": pc(inputs["L0_ffn_b1"], 8),
            "b2_pc": pc(inputs["L0_ffn_b2"], 8),
            "sb1_pc": pc(inputs["L0_ffn_sb1"], 1),
            "sb2_pc": pc(inputs["L0_ffn_sb2"], 1),
        }
    return {
        "gate": np.ascontiguousarray(np.asarray(inputs["L1_ffn_Wg"], np.float32)),
        "w1": np.ascontiguousarray(np.asarray(inputs["L1_ffn_W1"], np.float32)),
        "w2": np.ascontiguousarray(np.asarray(inputs["L1_ffn_W2"], np.float32)),
        "wy": np.ascontiguousarray(np.asarray(inputs["L1_ffn_W3"], np.float32)),
        "sw1": np.ascontiguousarray(np.asarray(inputs["L1_ffn_w1s"], np.float32)),
        "sw2": np.ascontiguousarray(np.asarray(inputs["L1_ffn_w2s"], np.float32)),
        "swy": np.ascontiguousarray(np.asarray(inputs["L1_ffn_w3s"], np.float32)),
    }



def emit_block2(b, g, aps):
    IC = 16
    sc = ln_scratch(b, "b2", 512)
    win = b.sb("b2_win", [128, DC, 4096], BF16)
    wout = b.sb("b2_wout", [128, IC, 1024], BF16)
    wx = b.sb("b2_wx", [128, IC, 32], BF16)
    wdt = b.sb("b2_wdt", [32, 2048], BF16)
    convw = b.sb("b2_convw", [128, 64], F32)
    convb = b.sb("b2_convb", [128, IC], F32)
    bdt = b.sb("b2_bdt", [128, IC], F32)
    maskt = b.sb("b2_mask", [128, 128], F32)
    xs_t = b.sb("b2_xs", [128, IC, 515], BF16)
    sz = b.sb("b2_sz", [128, IC, 512], BF16)
    xc = b.sb("b2_xc", [128, IC, 512], BF16)
    dt_sb = b.sb("b2_dt", [32, 512], BF16)
    cv = b.sb("b2_cv", [128, 512], F32)
    z0 = b.sb("b2_z0", [128, 512], F32)
    a1 = b.sb("b2_a1", [128, 512], F32)
    sp = b.sb("b2_sp", [128, 512], F32)

    sw = b.S("b2w")
    b.dma("gpsimd", win[:], aps["win_t"].rearrange("(k p) o -> p k o", p=128), sem=sw)
    b.dma("gpsimd", wout[:], aps["wout_t"].rearrange("(k p) o -> p k o", p=128), sem=sw)
    b.dma("gpsimd", wx[:], aps["wx_t"].rearrange("(k p) r -> p k r", p=128), sem=sw)
    wt = b.dma("gpsimd", wdt[:], aps["wdt_t"], sem=sw)
    sw2 = b.S("b2w2")
    b.dma("sync", convw[:], aps["convw_pc"], sem=sw2)
    b.dma("sync", convb[:], aps["convb_pc"], sem=sw2)
    b.dma("sync", bdt[:], aps["bdt_pc"], sem=sw2)
    wt2 = b.dma("sync", maskt[:], aps["mask"], sem=sw2)
    b.op("gpsimd", lambda e: e.memset(xs_t[:, :, 0:3], 0.0))
    mtok = b.dma("gpsimd", g["ident"][0:1, 0:1], g["ident"][0:1, 0:1], sem=b.S("b2ms"))
    for eng in ("tensor", "scalar", "vector"):
        b.wait(eng, sw, wt)
        b.wait(eng, sw2, wt2)
        b.wait(eng, b.S("b2ms"), mtok)

    def chain2(fn, eng="vector"):
        sem = {"vector": b.sdve, "scalar": b.sact}[eng]
        for s2 in (b.sdve, b.sact):
            b.wait(eng, s2, s2.n)
        return b.op(eng, fn, inc=sem)

    groups = [(0, HALO)] + [(HALO + 512 * k, 512) for k in range(8)]
    st_tok = None
    prev_T = None
    for gi, (off, T) in enumerate(groups):
        d = emit_ln(b, g, sc, aps["xres_in"][:, off:off + T], T,
                    sbanks=(0, 1, 2, 3), abanks=(4, 5, 6, 7))
        b.wait("tensor", b.sdve, d["xhat_dve"])
        if gi > 0:
            cvd = b.op("vector", lambda e, prev_T=prev_T: e.tensor_copy(
                xs_t[:, :, 0:3], xs_t[:, :, prev_T:prev_T + 3]), inc=b.sdve)
            b.wait("scalar", b.sdve, cvd)
        for oc in range(32):
            ib = oc % 4
            bank = b.use_bank(ib, "tensor")
            for kc in range(DC):
                b.op("tensor", lambda e, kc=kc, oc=oc, T=T, bank=bank: e.matmul(
                    bank[:, 0:T], win[:, kc, oc * 128:(oc + 1) * 128],
                    sc.xhat[:, kc, 0:T], start=(kc == 0), stop=(kc == DC - 1)),
                    inc=(b.spe if kc == DC - 1 else None))
            v = b.spe.n
            b.wait("scalar", b.spe, v)
            if oc < IC:
                va = b.op("scalar", lambda e, oc=oc, T=T, bank=bank: e.copy(
                    out=xs_t[:, oc, 3:3 + T], in_=bank[:, 0:T]), inc=b.sact)
                if gi == 0:
                    b.wait("vector", b.sact, va)
                    va2 = b.op("vector", lambda e, oc=oc, T=T: e.tensor_mul(
                        xs_t[:, oc, 3:3 + T], xs_t[:, oc, 3:3 + T],
                        maskt[:, 0:T]), inc=b.sdve)
                    b.release_bank(ib, b.sdve, va2)
                else:
                    b.release_bank(ib, b.sact, va)
            else:
                va = b.op("scalar", lambda e, oc=oc, T=T, bank=bank: e.activation(
                    out=sz[:, oc - IC, 0:T], in_=bank[:, 0:T], func=AF.Silu),
                    inc=b.sact)
                b.release_bank(ib, b.sact, va)
        xz_done_act = b.sact.n
        xz_done_dve = b.sdve.n
        sc.dep_xhat = [(b.spe, b.spe.n)]
        if gi == 0:
            prev_T = T
            continue
        b.wait("vector", b.sact, xz_done_act)
        for c in range(IC):
            b.wait("vector", b.sact, b.sact.n)
            b.op("vector", lambda e, c=c, T=T: e.tensor_scalar(
                cv[:, 0:T], xs_t[:, c, 0:T], convw[:, 4 * c:4 * c + 1], None, ALU.mult))
            for j in (1, 2, 3):
                b.op("vector", lambda e, c=c, j=j, T=T: e.scalar_tensor_tensor(
                    cv[:, 0:T], xs_t[:, c, j:j + T], convw[:, 4 * c + j:4 * c + j + 1],
                    cv[:, 0:T], ALU.mult, ALU.add))
            vd = b.op("vector", lambda e: e.engine_nop(), inc=b.sdve) if False else None
            vd = b.op("vector", lambda e, T=T: e.tensor_copy(cv[:, 0:T], cv[:, 0:T]),
                      inc=b.sdve)
            b.wait("scalar", b.sdve, vd)
            va = b.op("scalar", lambda e, c=c, T=T: e.activation(
                out=xc[:, c, 0:T], in_=cv[:, 0:T], func=AF.Silu,
                bias=convb[:, c:c + 1]), inc=b.sact)
        conv_done = b.sact.n
        b.wait("tensor", b.sact, conv_done)
        dbank = b.use_bank(4, "tensor")
        for kc in range(IC):
            b.op("tensor", lambda e, kc=kc, T=T, dbank=dbank: e.matmul(
                dbank[0:32, 0:T], wx[:, kc, :], xc[:, kc, 0:T],
                start=(kc == 0), stop=(kc == IC - 1)),
                inc=(b.spe if kc == IC - 1 else None))
        v = b.spe.n
        b.wait("scalar", b.spe, v)
        va = b.op("scalar", lambda e, T=T, dbank=dbank: e.copy(
            out=dt_sb[0:32, 0:T], in_=dbank[0:32, 0:T]), inc=b.sact)
        b.release_bank(4, b.sact, va)
        dt_done = va
        b.wait("tensor", b.sact, dt_done)
        for ec in range(IC):
            ib = 5 + ec % 2
            bank = b.use_bank(ib, "tensor")
            v = b.op("tensor", lambda e, ec=ec, T=T, bank=bank: e.matmul(
                bank[:, 0:T], wdt[0:32, ec * 128:(ec + 1) * 128], dt_sb[0:32, 0:T],
                start=True, stop=True), inc=b.spe)
            b.wait("vector", b.spe, v)
            chain2(lambda e, ec=ec, T=T, bank=bank: e.tensor_scalar(
                z0[:, 0:T], bank[:, 0:T], bdt[:, ec:ec + 1], None, ALU.add))
            chain2(lambda e, T=T: e.scalar_tensor_tensor(
                a1[:, 0:T], z0[:, 0:T], -1.0, z0[:, 0:T], ALU.mult, ALU.min))
            chain2(lambda e, T=T: e.activation(
                out=a1[:, 0:T], in_=a1[:, 0:T], func=AF.Exp), "scalar")
            chain2(lambda e, T=T: e.activation(
                out=a1[:, 0:T], in_=a1[:, 0:T], func=AF.Ln, bias=1.0), "scalar")
            chain2(lambda e, T=T: e.scalar_tensor_tensor(
                sp[:, 0:T], z0[:, 0:T], 0.0, a1[:, 0:T], ALU.max, ALU.add))
            chain2(lambda e, ec=ec, T=T: e.tensor_mul(
                sp[:, 0:T], sp[:, 0:T], xc[:, ec, 0:T]))
            vd = chain2(lambda e, ec=ec, T=T: e.tensor_mul(
                xc[:, ec, 0:T], sp[:, 0:T], sz[:, ec, 0:T]))
            b.release_bank(ib, b.sdve, vd)
        y2_done = b.sdve.n
        b.wait("tensor", b.sdve, y2_done)
        for oc in range(DC):
            ib = oc % 4
            bank = b.use_bank(ib, "tensor")
            for ec in range(IC):
                b.op("tensor", lambda e, ec=ec, oc=oc, T=T, bank=bank: e.matmul(
                    bank[:, 0:T], wout[:, ec, oc * 128:(oc + 1) * 128],
                    xc[:, ec, 0:T], start=(ec == 0), stop=(ec == IC - 1)),
                    inc=(b.spe if ec == IC - 1 else None))
            v = b.spe.n
            b.wait("vector", b.spe, v)
            vd = b.op("vector", lambda e, oc=oc, T=T, bank=bank: e.tensor_add(
                sc.xg[:, oc, 0:T], bank[:, 0:T], sc.xg[:, oc, 0:T]), inc=b.sdve)
            b.release_bank(ib, b.sdve, vd)
        res_done = b.sdve.n
        b.wait("sync", b.sdve, res_done)
        st_tok = b.dma("sync", fm(aps["xres_out"][:, off - HALO:off - HALO + T]),
                       sc.xg[:, :, 0:T], sem=b.S("b2st"))
        sc.dep_xg += [(b.S("b2st"), st_tok), (b.sdve, res_done)]
        prev_T = T
    return {"out_store": st_tok}


def build_b2():
    b = B()
    nc = b.nc
    aps = {}
    aps["xres_in"] = nc.dram_tensor("xres_in", [D, NT], F32, kind="ExternalInput").ap()
    aps["win_t"] = nc.dram_tensor("win_t", [1024, 4096], F32, kind="ExternalInput").ap()
    aps["wout_t"] = nc.dram_tensor("wout_t", [2048, 1024], F32, kind="ExternalInput").ap()
    aps["wx_t"] = nc.dram_tensor("wx_t", [2048, 32], F32, kind="ExternalInput").ap()
    aps["wdt_t"] = nc.dram_tensor("wdt_t", [32, 2048], F32, kind="ExternalInput").ap()
    aps["convw_pc"] = nc.dram_tensor("convw_pc", [128, 64], F32, kind="ExternalInput").ap()
    aps["convb_pc"] = nc.dram_tensor("convb_pc", [128, 16], F32, kind="ExternalInput").ap()
    aps["bdt_pc"] = nc.dram_tensor("bdt_pc", [128, 16], F32, kind="ExternalInput").ap()
    aps["mask"] = nc.dram_tensor("mask", [128, 128], F32, kind="ExternalInput").ap()
    aps["xres_out"] = nc.dram_tensor("xres_out", [D, OWN], F32, kind="ExternalOutput").ap()
    g = emit_globals(b)
    emit_block2(b, g, aps)
    b.final_dma_waits("sync")
    b.final_dma_waits("gpsimd")
    return b.finalize()


def prep_block2(inputs, core):
    pc16 = lambda v: np.ascontiguousarray(
        np.asarray(v, np.float32).reshape(16, 128).T)
    mask = (np.ones((128, 128), np.float32) if core % 2 == 1
            else np.zeros((128, 128), np.float32))
    cw = np.asarray(inputs["L1_attn_conv_w"], np.float32)[:, 0, :]
    convw = cw.reshape(16, 128, 4).transpose(1, 0, 2).reshape(128, 64)
    return {
        "win_t": np.ascontiguousarray(np.asarray(inputs["L1_attn_W_in"], np.float32).T),
        "wout_t": np.ascontiguousarray(np.asarray(inputs["L1_attn_W_out"], np.float32).T),
        "wx_t": np.ascontiguousarray(np.asarray(inputs["L1_attn_W_x"], np.float32).T[:, 0:32]),
        "wdt_t": np.ascontiguousarray(np.asarray(inputs["L1_attn_W_dt"], np.float32).T),
        "convw_pc": np.ascontiguousarray(convw),
        "convb_pc": pc16(inputs["L1_attn_conv_b"]),
        "bdt_pc": pc16(inputs["L1_attn_b_dt"]),
        "mask": mask,
    }



def prep_block0(inputs, core):
    x = np.asarray(inputs["L0_attn_x"])
    c = core // 2
    xb = np.ascontiguousarray(x[c].T)
    if core % 2 == 0:
        xmain = np.concatenate([np.zeros((D, HALO), np.float32), xb[:, :OWN]], axis=1)
    else:
        xmain = xb[:, OWN - HALO:SEQ]
    return {
        "xkv": xb,
        "xmain": np.ascontiguousarray(xmain),
        "wq": np.ascontiguousarray(np.asarray(inputs["L0_attn_Wq"]).T),
        "wk": np.ascontiguousarray(np.asarray(inputs["L0_attn_Wk"]).T),
        "wv": np.ascontiguousarray(np.asarray(inputs["L0_attn_Wv"]).T),
        "wo": np.ascontiguousarray(np.asarray(inputs["L0_attn_Wo"]).T),
        "bq_r": np.asarray(inputs["L0_attn_bq"]).reshape(1, D).astype(np.float32),
        "bk_r": np.asarray(inputs["L0_attn_bk"]).reshape(1, D).astype(np.float32),
        "bv_r": np.asarray(inputs["L0_attn_bv"]).reshape(1, D).astype(np.float32),
        "bo_pc": np.ascontiguousarray(
            np.asarray(inputs["L0_attn_bo"]).reshape(DC, 128).T).astype(np.float32),
    }


def build_b0():
    b = B()
    nc = b.nc
    aps = {}
    aps["xkv"] = nc.dram_tensor("xkv", [D, SEQ], F32, kind="ExternalInput").ap()
    aps["xmain"] = nc.dram_tensor("xmain", [D, NT], F32, kind="ExternalInput").ap()
    for nm in ("wq", "wk", "wv", "wo"):
        aps[nm] = nc.dram_tensor(nm, [D, D], F32, kind="ExternalInput").ap()
    for nm in ("bq_r", "bk_r", "bv_r"):
        aps[nm] = nc.dram_tensor(nm, [1, D], F32, kind="ExternalInput").ap()
    aps["bo_pc"] = nc.dram_tensor("bo_pc", [128, DC], F32, kind="ExternalInput").ap()
    aps["xres_out"] = nc.dram_tensor("xres_out", [D, NT], F32, kind="ExternalOutput").ap()
    g = emit_globals(b)
    emit_block0(b, g, aps)
    b.final_dma_waits("sync")
    b.final_dma_waits("gpsimd")
    return b.finalize()


_PROG_CACHE = {}


def _get_prog(name):
    if name not in _PROG_CACHE:
        _PROG_CACHE[name] = {"b0": build_b0, "b1": lambda: build_ffn("b1"),
                             "b2": build_b2, "b3": lambda: build_ffn("b3")}[name]()
    return _PROG_CACHE[name]


def _halo_cat(xb, half):
    if half == 0:
        return np.ascontiguousarray(
            np.concatenate([np.zeros((D, HALO), np.float32), xb[:, :OWN]], axis=1))
    return np.ascontiguousarray(xb[:, OWN - HALO:SEQ])


def kernel(**inputs):
    in_maps = [prep_block0(inputs, c) for c in range(NCORES)]
    res = run_bass_kernel_spmd(_get_prog("b0"), in_maps, core_ids=list(range(NCORES)))
    xa = [np.asarray(r["xres_out"]) for r in res.results]
    in_maps = []
    for core in range(NCORES):
        m = prep_ffn(inputs, core, "b1")
        m["xres_in"] = xa[core]
        in_maps.append(m)
    res = run_bass_kernel_spmd(_get_prog("b1"), in_maps, core_ids=list(range(NCORES)))
    xb_ = [np.asarray(r["out_hbm"]) for r in res.results]
    in_maps = []
    for core in range(NCORES):
        m = prep_block2(inputs, core)
        m["xres_in"] = xb_[core]
        in_maps.append(m)
    res = run_bass_kernel_spmd(_get_prog("b2"), in_maps, core_ids=list(range(NCORES)))
    xc_ = [np.asarray(r["xres_out"]) for r in res.results]
    in_maps = []
    for core in range(NCORES):
        m = prep_ffn(inputs, core, "b3")
        m["xres_in"] = xc_[core]
        in_maps.append(m)
    res = run_bass_kernel_spmd(_get_prog("b3"), in_maps, core_ids=list(range(NCORES)))
    full = np.zeros((4, SEQ, D), np.float32)
    for core in range(NCORES):
        c, half = core // 2, core % 2
        full[c, half * OWN:(half + 1) * OWN, :] = np.asarray(
            res.results[core]["out_hbm"]).T
    return full
